# version 31
# baseline (speedup 1.0000x reference)
import numpy as np
import concourse.bass as bass
import concourse.mybir as mybir
from concourse.bass_utils import run_bass_kernel_spmd

F32 = mybir.dt.float32
BF16 = mybir.dt.bfloat16
AL = mybir.AluOpType
ACT = mybir.ActivationFunctionType
AX = mybir.AxisListType

NCORES = 8
T = 2048
D = 2048
FI = 8192
NT = T // 128
NE = D // 128
NF = 32
NIF = FI // 128
C_GATE = 8.0
EPS = 1e-5
SCALE_LO = 2.0 ** -30
SCALE_HI = 2.0 ** 30


def _bc(ap_blk, inner):
    import dataclasses
    return dataclasses.replace(ap_blk, ap=list(ap_blk.ap) + [[0, inner]])


def build_program(DEBUG=False, SLABS=None, HALVES=(0, 1)):
    nc = bass.Bass("TRN2", target_bir_lowering=False,
                   detect_race_conditions=False)

    xs_d = nc.dram_tensor("xs", [T, D], F32, kind="ExternalInput").ap()
    wint_d = nc.dram_tensor("wint", [NF, 128, NE * 128], F32,
                            kind="ExternalInput").ap()
    wg_d = nc.dram_tensor("wg", [FI // 8, D], F32, kind="ExternalInput").ap()
    wu_d = nc.dram_tensor("wu", [FI // 8, D], F32, kind="ExternalInput").ap()
    wd_d = nc.dram_tensor("wd", [D // 8, FI], F32, kind="ExternalInput").ap()
    afull_d = nc.dram_tensor("afull", [128, 16], F32, kind="ExternalInput").ap()
    cmask_d = nc.dram_tensor("cmask", [128, 8], F32, kind="ExternalInput").ap()
    eye_d = nc.dram_tensor("eye", [128, 128], F32, kind="ExternalInput").ap()
    out_d = nc.dram_tensor("out", [T, D], F32, kind="ExternalOutput").ap()
    if DEBUG:
        dbg_h_d = nc.dram_tensor("dbg_h", [T, D], F32, kind="ExternalOutput").ap()
        dbg_wg_d = nc.dram_tensor("dbg_wg", [NIF, NE, 128, 128], BF16,
                                  kind="ExternalOutput").ap()
        dbg_xnq_d = nc.dram_tensor("dbg_xnq", [NE, NT, 128, 128], BF16,
                                   kind="ExternalOutput").ap()
        dbg_tqt_d = nc.dram_tensor("dbg_tqt", [128, 4096], BF16,
                                   kind="ExternalOutput").ap()
        dbg_ga_d = nc.dram_tensor("dbg_ga", [128, T], F32,
                                  kind="ExternalOutput").ap()
        dbg_gx_d = nc.dram_tensor("dbg_gx", [128, T], F32,
                                  kind="ExternalOutput").ap()
        dbg_xnt_d = nc.dram_tensor("dbg_xnt", [128, NE * T], F32,
                                   kind="ExternalOutput").ap()
        dbg_st_d = nc.dram_tensor("dbg_st", [128, 64], F32,
                                  kind="ExternalOutput").ap()
        dbg_int_d = nc.dram_tensor("dbg_int", [128, 512], F32,
                                   kind="ExternalOutput").ap()
        dbg_silu_d = nc.dram_tensor("dbg_silu", [128, 512], F32,
                                    kind="ExternalOutput").ap()
        dbg_tq_d = nc.dram_tensor("dbg_tq", [128, 512], BF16,
                                  kind="ExternalOutput").ap()
        dbg_tqt2_d = nc.dram_tensor("dbg_tqt2", [128, 4096], BF16,
                                    kind="ExternalOutput").ap()
        dbg_wd_d = nc.dram_tensor("dbg_wd", [NE, NIF, 128, 128], BF16,
                                  kind="ExternalOutput").ap()
        dbg_q_d = nc.dram_tensor("dbg_q", [128, 4096], F32,
                                 kind="ExternalOutput").ap()

    xh_d = nc.dram_tensor("xh_sp", [T, D], F32, kind="Internal").ap()
    xnqt_d = nc.dram_tensor("xnqt_sp", [NE, NT, 128, 128], BF16,
                            kind="Internal").ap()
    wgqt_d = nc.dram_tensor("wgqt_sp", [8, 128, 2 * 8192], BF16,
                            kind="Internal", addr_space="Shared").ap()
    wuqt_d = nc.dram_tensor("wuqt_sp", [8, 128, 2 * 8192], BF16,
                            kind="Internal", addr_space="Shared").ap()
    wdqt_d = nc.dram_tensor("wdqt_sp", [NE, NIF, 128, 128], BF16,
                            kind="Internal", addr_space="Shared").ap()
    wgsh_d = nc.dram_tensor("wgsh_sp", [128, 2 * 8192], BF16,
                            kind="Internal").ap()
    wush_d = nc.dram_tensor("wush_sp", [128, 2 * 8192], BF16,
                            kind="Internal").ap()
    wdsh_d = nc.dram_tensor("wdsh_sp", [NE // 8, NIF, 128, 128], BF16,
                            kind="Internal").ap()
    csnd_d = [nc.dram_tensor(f"csnd{k}", [1, 128], F32, kind="Internal").ap()
              for k in range(NE)]
    cgat_d = [nc.dram_tensor(f"cgat{k}", [8, 128], F32, kind="Internal",
                          addr_space="Shared").ap()
              for k in range(NE)]

    from contextlib import ExitStack
    perm = ExitStack()
    E = perm.enter_context

    eye_s = E(nc.sbuf_tensor("eye_s", [128, 128], F32))
    eyeb_s = E(nc.sbuf_tensor("eyeb_s", [128, 128], BF16))
    afull_s = E(nc.sbuf_tensor("afull_s", [128, 16], F32))
    cmask_s = E(nc.sbuf_tensor("cmask_s", [128, 8], F32))
    epsb_s = E(nc.sbuf_tensor("epsb_s", [128, 1], F32))

    s_ld = E(nc.semaphore("s_ld"))
    s_sp = E(nc.semaphore("s_sp"))
    s_v = E(nc.semaphore("s_v"))
    s_sc = E(nc.semaphore("s_sc"))
    s_t = E(nc.semaphore("s_t"))
    s_cc = E(nc.semaphore("s_cc"))
    s_rt = E(nc.semaphore("s_rt"))
    s_g = E(nc.semaphore("s_g"))
    s_o = E(nc.semaphore("s_o"))
    s_x0 = E(nc.semaphore("s_x0"))
    s_x1 = E(nc.semaphore("s_x1"))

    blk = E(nc.Block())
    cnt = {"ld": 0, "sp": 0, "v": 0, "sc": 0, "t": 0, "cc": 0, "rt": 0,
           "g": 0, "o": 0, "x0": 0, "x1": 0}

    def body(v):
        sy, sc, te, gp = nc.sync, nc.scalar, nc.tensor, nc.gpsimd

        def LD(inst):
            inst.then_inc(s_ld, 16); cnt["ld"] += 16; return cnt["ld"]

        def SP(inst):
            inst.then_inc(s_sp, 16); cnt["sp"] += 16; return cnt["sp"]

        def V(inst):
            inst.then_inc(s_v, 1); cnt["v"] += 1; return cnt["v"]

        def SC(inst):
            inst.then_inc(s_sc, 1); cnt["sc"] += 1; return cnt["sc"]

        def TE(inst):
            inst.then_inc(s_t, 1); cnt["t"] += 1; return cnt["t"]

        def GP(inst):
            inst.then_inc(s_g, 1); cnt["g"] += 1; return cnt["g"]

        s_xp = (s_x0, s_x1)

        def LO(inst):
            inst.then_inc(s_o, 16); cnt["o"] += 16; return cnt["o"]

        def LX(p, inst):
            inst.then_inc(s_xp[p], 16); cnt["x%d" % p] += 16
            return cnt["x%d" % p]

        def RT(inst):
            inst.then_inc(s_rt, 1); cnt["rt"] += 1
            v.wait_ge(s_rt, cnt["rt"])

        def scale_ladder(amax_ap, nb, rep, gb_t, sc_t, rb_ap, mfrac, q):
            W = rep * nb
            va = sc_t[:, 0:W]
            fac = sc_t[:, W:2 * W]
            cb = gb_t[:, W:2 * W]
            gb = gb_t[:, 0:nb]
            rb = rb_ap[:, 0:W]
            if rep > 1:
                va3 = va.rearrange("p (r n) -> p r n", r=rep)
                src = _bc(amax_ap, rep).rearrange("p n r -> p r n")
            else:
                va3 = va
                src = amax_ap
            q.memset(rb, float(2.0 ** (30 + 2 + mfrac)))
            q.tensor_scalar(va3, src, SCALE_LO, SCALE_HI, AL.max, AL.min)
            q.tensor_scalar(va, va, 2.0 ** 30, None, AL.mult)
            for j in (16, 16, 16, 8, 4, 2, 1):
                q.tensor_scalar(cb, va, float(2.0 ** j), None, AL.is_ge)
                q.tensor_scalar(fac, cb, float(2.0 ** (-j) - 1.0), 1.0,
                                AL.mult, AL.add)
                q.tensor_tensor(va, va, fac, AL.mult)
                q.tensor_tensor(rb, rb, fac, AL.mult)
            cnt["rt"] += 1
            v.reciprocal(gb, rb[:, 0:nb]).then_inc(s_rt, 1)
            return gb, W, cnt["rt"]

        def quant(x_ap, q_ap, mfrac, qs, tie, eng="v", x_guard=None):
            F = x_ap.shape[-1]
            nb = F // 32
            q = v if eng == "v" else gp
            QM = V if eng == "v" else GP
            qsem = s_v if eng == "v" else s_g
            sg = qs["sg"][:, 0:F]; av = qs["av"][:, 0:F]
            m2 = qs["m2"][:, 0:F]; m4 = qs["m4"][:, 0:F]
            t0 = qs["t0"][:, 0:F]; t1 = qs["t1"][:, 0:F]
            amax = qs["amax"][:, 0:nb]
            cap = float(2.0 ** (mfrac + 1))
            if x_guard is None:
                sc.wait_ge(s_v, cnt["v"])
            elif x_guard:
                sc.wait_ge(s_v, x_guard)
            sgp = qs.get("_sgp")
            if sgp:
                sc.wait_ge(s_v, sgp)
            prev = qs.get("_last")
            if prev is not None and prev[0] is s_g:
                sc.wait_ge(s_g, prev[1])
                v.wait_ge(s_g, prev[1])
            sgn_c = SC(sc.activation(sg, x_ap, ACT.Sign))
            rep = 1 if eng == "gp" else max(1, (128 + nb - 1) // nb)
            W = rep * nb
            rbW = qs["blk"][:, 2 * W:3 * W]
            x3 = x_ap.rearrange("p (b k) -> p b k", k=32)
            cnt["rt"] += 1
            v.tensor_reduce(amax, x3, AX.X, AL.max,
                            apply_absolute_value=True).then_inc(s_rt, 1)
            rt_amax = cnt["rt"]
            q.wait_ge(s_sc, sgn_c)
            q.tensor_tensor(av, x_ap, sg, AL.mult)
            q.wait_ge(s_rt, rt_amax)
            gb, W, rt_gb = scale_ladder(amax, nb, rep, qs["gbsc"],
                                        qs["lsc"], rbW, mfrac, q)
            rb = rbW[:, 0:nb]
            q.tensor_tensor(av, av, _bc(rb, 32), AL.mult)
            q.tensor_scalar(m2, av, cap, None, AL.is_ge)
            q.tensor_scalar(m4, av, 2.0 * cap, None, AL.is_ge)
            q.tensor_scalar(t0, m2, -0.5, 1.0, AL.mult, AL.add)
            q.tensor_scalar(t1, m4, -0.5, 1.0, AL.mult, AL.add)
            q.tensor_tensor(t0, t0, t1, AL.mult)
            q.tensor_tensor(t1, av, t0, AL.mult)
            q.tensor_copy(av, t1)
            for j in [2 ** b for b in range(mfrac, -1, -1)]:
                q.tensor_scalar(t0, av, float(j), None, AL.is_ge)
                if eng == "v":
                    q.scalar_tensor_tensor(av, t0, float(-j), av,
                                           AL.mult, AL.add)
                elif j > 1:
                    q.tensor_scalar(t0, t0, float(j), None, AL.mult)
                    q.tensor_tensor(av, av, t0, AL.subtract)
                else:
                    q.tensor_tensor(av, av, t0, AL.subtract)
            q.tensor_tensor(t1, t1, av, AL.subtract)
            if tie:
                t2 = qs["t2"][:, 0:F]
                q.tensor_scalar(t2, av, 0.5, None, AL.is_equal)
                q.tensor_tensor(t2, t2, t0, AL.mult)
                q.tensor_scalar(av, av, 0.5, None, AL.is_gt)
                q.tensor_tensor(t1, t1, av, AL.add)
                q.tensor_tensor(t1, t1, t2, AL.add)
            else:
                q.tensor_scalar(av, av, 0.5, None, AL.is_gt)
                q.tensor_tensor(t1, t1, av, AL.add)
            q.tensor_scalar(t1, t1, cap, None, AL.min)
            q.tensor_scalar(av, t1, cap, None, AL.is_equal)
            q.tensor_tensor(t0, av, m4, AL.mult)
            q.tensor_tensor(t1, t1, t0, AL.subtract)
            q.tensor_scalar(m2, m2, 1.0, None, AL.add)
            q.tensor_scalar(m4, m4, 1.0, None, AL.add)
            q.tensor_tensor(m2, m2, m4, AL.mult)
            q.wait_ge(s_rt, rt_gb)
            q.tensor_tensor(m2, m2, _bc(gb, 32), AL.mult)
            q.tensor_tensor(t1, t1, m2, AL.mult)
            mk = QM(q.tensor_tensor(q_ap, t1, sg, AL.mult))
            qs["_last"] = (qsem, mk)
            if eng == "v":
                qs["_sgp"] = mk
            return qsem, mk

        LD(sy.dma_start(out=eye_s[:, :], in_=eye_d))
        LD(sy.dma_start(out=afull_s[:, :], in_=afull_d))
        c_consts = LD(sy.dma_start(out=cmask_s[:, :], in_=cmask_d))
        sc.wait_ge(s_ld, c_consts)
        sp1 = SC(sc.activation(afull_s[:, :], afull_s[:, :], ACT.Exp))
        sc.wait_ge(s_sc, sp1)
        SC(sc.activation(afull_s[:, :], afull_s[:, :], ACT.Ln, bias=1.0))
        c_afull_sc = cnt["sc"]
        v.wait_ge(s_ld, c_consts)
        RT(v.memset(epsb_s[:, :], EPS))
        V(v.tensor_copy(eyeb_s[:, :], eye_s[:, :]))
        c_eyeb_v = cnt["v"]

        wsc = ExitStack()
        EW = wsc.enter_context
        wst = [EW(nc.sbuf_tensor(f"wst{i}", [128, D], F32)) for i in range(2)]
        wq_s = EW(nc.sbuf_tensor("wq_s", [128, D], BF16))
        qsW = {k: EW(nc.sbuf_tensor(f"qw_{k}", [128, w], F32))[:, :]
               for k, w in [("sg", D), ("av", D), ("m2", D), ("m4", D),
                            ("t0", D), ("t1", D), ("t2", D), ("amax", 64),
                            ("gbsc", 512), ("lsc", 512), ("blk", 768)]}
        tev = [EW(nc.sbuf_tensor(f"tev{i}", [128, 128], BF16)) for i in range(8)]
        sgWb = EW(nc.sbuf_tensor("qw_sgB", [128, D], F32))
        qsW2 = dict(qsW)
        qsW2["sg"] = sgWb[:, :]
        pw = [EW(nc.psum_tensor(f"pw{i}", [128, 128], BF16)) for i in range(2)]

        wjobs = []

        def _gu_dst(sh_d, rl, e):
            c0 = (rl // 4) * 8192 + e * 512 + (rl % 4) * 128
            return sh_d[:, c0:c0 + 128]

        for rl in range(NIF // 8):
            wjobs.append((wg_d[128 * rl:128 * (rl + 1), :],
                          [_gu_dst(wgsh_d, rl, e) for e in range(NE)]))
        for rl in range(NIF // 8):
            wjobs.append((wu_d[128 * rl:128 * (rl + 1), :],
                          [_gu_dst(wush_d, rl, e) for e in range(NE)]))
        for hl in range(D // 128 // 8):
            for s in range(FI // D):
                wjobs.append((wd_d[128 * hl:128 * (hl + 1), D * s:D * (s + 1)],
                              [wdsh_d[hl, s * 16 + e] for e in range(NE)]))

        cons_mark = []
        ev_mark = []
        sp_mark = []
        tr_mark = []
        for i, (src, dsts) in enumerate(wjobs):
            if i >= 2:
                sy.wait_ge(*cons_mark[i - 2])
            ldc = LX(i % 2, sy.dma_start(out=wst[i % 2][:, :], in_=src))
            sc.wait_ge(s_xp[i % 2], ldc)
            v.wait_ge(s_xp[i % 2], ldc)
            if i >= 1:
                v.wait_ge(s_t, tr_mark[i - 1])
            qsem, qlast = quant(wst[i % 2][:, :], wq_s[:, :], 1,
                                (qsW, qsW2)[i % 2], tie=True, x_guard=0)
            cons_mark.append((qsem, qlast))
            te.wait_ge(qsem, qlast)
            for e in range(NE):
                n = 16 * i + e
                if n >= 2:
                    te.wait_ge(s_sc, ev_mark[n - 2])
                tc = TE(te.transpose(pw[n % 2][:, :],
                                     wq_s[:, 128 * e:128 * (e + 1)],
                                     eyeb_s[:, :]))
                sc.wait_ge(s_t, tc)
                if n >= 8:
                    sc.wait_ge(s_sp, sp_mark[n - 8])
                em = SC(sc.activation(tev[n % 8][:, :], pw[n % 2][:, :],
                                      ACT.Identity))
                ev_mark.append(em)
                gp.wait_ge(s_sc, em)
                sp_mark.append(SP(gp.dma_start(out=dsts[e],
                                               in_=tev[n % 8][:, :])))
            tr_mark.append(cnt["t"])
        w_end_v, w_end_t, w_end_sp = cnt["v"], cnt["t"], cnt["sp"]
        w_end_g = cnt["g"]
        w_end_sc = cnt["sc"]
        wsc.close()

        s0a = ExitStack()
        E0A = s0a.enter_context
        xnT = E0A(nc.sbuf_tensor("xnT", [128, NE * T], F32))

        s0 = ExitStack()
        E0 = s0.enter_context
        x_t = [E0(nc.sbuf_tensor(f"x_t{i}", [128, D], F32)) for i in range(2)]
        xn_t = E0(nc.sbuf_tensor("xn_t", [128, D], F32))
        xnq_t = [E0(nc.sbuf_tensor(f"xnq_t{i}", [128, D], BF16))
                 for i in range(2)]
        sums = E0(nc.sbuf_tensor("sums", [128, 16], F32))
        sqs = E0(nc.sbuf_tensor("sqs", [128, 16], F32))
        rstd = E0(nc.sbuf_tensor("rstd", [128, 16], F32))
        nbias = E0(nc.sbuf_tensor("nbias", [128, 16], F32))
        stt16 = E0(nc.sbuf_tensor("stt16", [128, 16], F32))
        sqscr = E0(nc.sbuf_tensor("sqscr", [128, D // 2], F32))
        qs0 = {k: E0(nc.sbuf_tensor(f"q0_{k}", [128, w], F32))[:, :]
               for k, w in [("sg", 1024), ("av", 1024), ("m2", 1024),
                            ("m4", 1024), ("t0", 1024), ("t1", 1024), ("t2", 1024),
                            ("amax", 32), ("gbsc", 512), ("lsc", 512),
                            ("blk", 768)]}
        sg0b = E0(nc.sbuf_tensor("q0_sgB", [128, 1024], F32))
        qs0b = dict(qs0)
        qs0b["sg"] = sg0b[:, :]
        p0f = [E0(nc.psum_tensor(f"p0f{i}", [128, 128], F32)) for i in range(2)]
        p0b = [E0(nc.psum_tensor(f"p0b{i}", [128, 128], BF16)) for i in range(2)]
        ev0 = [E0(nc.sbuf_tensor(f"ev0{i}", [128, 128], BF16)) for i in range(8)]

        st_mark = []
        for j in range(NT):
            if j < 2:
                sy.wait_ge(s_v, w_end_v)
                sy.wait_ge(s_t, w_end_t)
                sy.wait_ge(s_sp, w_end_sp)
                sy.wait_ge(s_g, w_end_g)
                sy.wait_ge(s_sc, w_end_sc)
            else:
                sy.wait_ge(s_v, st_mark[j - 2])
            ldc = LX(j % 2, sy.dma_start(out=x_t[j % 2][:, :],
                                         in_=xs_d[128 * j:128 * (j + 1), :]))
            v.wait_ge(s_xp[j % 2], ldc)
            v.tensor_reduce(sums[:, j:j + 1], x_t[j % 2][:, :], AX.X, AL.add)
            v.tensor_tensor(sqscr[:, :], x_t[j % 2][:, 0:D // 2],
                            x_t[j % 2][:, 0:D // 2], AL.mult)
            RT(v.tensor_reduce(sqs[:, j:j + 1], sqscr[:, :], AX.X, AL.add))
            v.tensor_tensor(sqscr[:, :], x_t[j % 2][:, D // 2:D],
                            x_t[j % 2][:, D // 2:D], AL.mult)
            RT(v.tensor_reduce(stt16[:, j:j + 1], sqscr[:, :], AX.X, AL.add))
            st_mark.append(V(v.tensor_tensor(sqs[:, j:j + 1], sqs[:, j:j + 1],
                                             stt16[:, j:j + 1], AL.add)))
        v.wait_ge(s_v, st_mark[-1])
        RT(v.tensor_scalar(sums[:, :], sums[:, :], 1.0 / D, None, AL.mult))
        RT(v.tensor_scalar(sqs[:, :], sqs[:, :], 1.0 / D, None, AL.mult))
        RT(v.tensor_tensor(stt16[:, :], sums[:, :], sums[:, :], AL.mult))
        c_var = V(v.tensor_tensor(sqs[:, :], sqs[:, :], stt16[:, :],
                                  AL.subtract))
        sc.wait_ge(s_v, c_var)
        c_sqv = SC(sc.activation(stt16[:, :], sqs[:, :], ACT.Sqrt,
                                 bias=epsb_s[:, 0:1]))
        v.wait_ge(s_sc, c_sqv)
        RT(v.reciprocal(rstd[:, :], stt16[:, :]))
        RT(v.scalar_tensor_tensor(nbias[:, :], sums[:, :], -1.0, rstd[:, :],
                                  AL.mult, AL.mult))
        c_stats_rt = cnt["rt"]

        evf_mark = []
        evb_mark = []
        spq_mark = []
        xn_tr_mark = []
        bf_mark = []
        pq_mark = []
        cxn_mark = []

        def emit_bf16(jj):
            te.wait_ge(*pq_mark[jj])
            for e in range(NE):
                n = 16 * jj + e
                if n >= 2:
                    te.wait_ge(s_sc, evb_mark[n - 2])
                tc = TE(te.transpose(p0b[n % 2][:, :],
                                     xnq_t[jj % 2][:, 128 * e:128 * (e + 1)],
                                     eyeb_s[:, :]))
                sc.wait_ge(s_t, tc)
                if n >= 8:
                    sc.wait_ge(s_sp, spq_mark[n - 8])
                em = SC(sc.activation(ev0[n % 8][:, :], p0b[n % 2][:, :],
                                      ACT.Identity))
                evb_mark.append(em)
                gp.wait_ge(s_sc, em)
                spq_mark.append(SP(gp.dma_start(out=xnqt_d[e, jj],
                                                in_=ev0[n % 8][:, :])))
            bf_mark.append(cnt["t"])

        for j in range(NT):
            if j == 0:
                sy.wait_ge(s_v, st_mark[-1])
            if j >= 2:
                sy.wait_ge(s_sc, cxn_mark[j - 2])
            ldc = LX(j % 2, sy.dma_start(out=x_t[j % 2][:, :],
                                         in_=xs_d[128 * j:128 * (j + 1), :]))
            sc.wait_ge(s_xp[j % 2], ldc)
            sc.wait_ge(s_rt, c_stats_rt)
            if j > 0:
                sc.wait_ge(s_t, xn_tr_mark[j - 1])
                sc.wait_ge(*pq_mark[j - 1])
            c_xn = SC(sc.activation(xn_t[:, :], x_t[j % 2][:, :], ACT.Identity,
                                    bias=nbias[:, j:j + 1],
                                    scale=rstd[:, j:j + 1]))
            cxn_mark.append(c_xn)
            if j > 0:
                emit_bf16(j - 1)
            v.wait_ge(s_sc, c_xn)
            if j >= 2:
                v.wait_ge(s_t, bf_mark[j - 2])
            quant(xn_t[:, 0:1024], xnq_t[j % 2][:, 0:1024], 3, qs0,
                  tie=True, x_guard=0)
            pq = quant(xn_t[:, 1024:2048], xnq_t[j % 2][:, 1024:2048], 3,
                       qs0b, tie=True, x_guard=0)
            pq_mark.append(pq)
            te.wait_ge(s_sc, c_xn)
            for e in range(NE):
                n = 16 * j + e
                if n >= 2:
                    te.wait_ge(s_sc, evf_mark[n - 2])
                tc = TE(te.transpose(p0f[n % 2][:, :],
                                     xn_t[:, 128 * e:128 * (e + 1)],
                                     eye_s[:, :]))
                sc.wait_ge(s_t, tc)
                evf_mark.append(SC(sc.activation(
                    xnT[:, e * T + 128 * j:e * T + 128 * (j + 1)],
                    p0f[n % 2][:, :], ACT.Identity)))
            xn_tr_mark.append(cnt["t"])
        emit_bf16(NT - 1)
        p0_end = (cnt["v"], cnt["t"], cnt["sp"], cnt["sc"])
        p0_g = cnt["g"]
        gp.wait_ge(s_sp, w_end_sp)
        for src_t, dst_t in ((wgsh_d, wgqt_d), (wush_d, wuqt_d),
                             (wdsh_d, wdqt_d)):
            gp.collective_compute(
                "AllGather", AL.bypass,
                replica_groups=[list(range(NCORES))],
                ins=[src_t], outs=[dst_t],
            ).then_inc(s_cc, 1)
            cnt["cc"] += 1
        wq_cc = cnt["cc"]
        if DEBUG:
            gp.wait_ge(s_v, p0_end[0])
            gp.wait_ge(s_sc, p0_end[3])
            SP(gp.dma_start(out=dbg_xnt_d, in_=xnT[:, :]))
            SP(gp.dma_start(out=dbg_st_d[:, 0:16], in_=sums[:, :]))
            SP(gp.dma_start(out=dbg_st_d[:, 16:32], in_=sqs[:, :]))
            SP(gp.dma_start(out=dbg_st_d[:, 32:48], in_=rstd[:, :]))
            SP(gp.dma_start(out=dbg_st_d[:, 48:64], in_=nbias[:, :]))
        s0.close()

        sA = ExitStack()
        EA = sA.enter_context
        wbuf0 = EA(nc.sbuf_tensor("wbuf0", [128, 2 * D], F32))
        zeros_s = EA(nc.sbuf_tensor("zeros_s", [128, T // 4], F32))
        gx_b = EA(nc.sbuf_tensor("gx_b", [128, 2 * T], F32))
        ga_b = EA(nc.sbuf_tensor("ga_b", [128, 2 * T], F32))
        h_b = [EA(nc.sbuf_tensor(f"h_b{i}", [128, T], F32)) for i in range(2)]
        ac_s = EA(nc.sbuf_tensor("ac_s", [128, T], F32))
        gat_s = EA(nc.sbuf_tensor("gat_s", [8, 128], F32))
        gm_s = EA(nc.sbuf_tensor("gm_s", [128, 8], F32))
        ceff_s = EA(nc.sbuf_tensor("ceff_s", [128, 1], F32))
        hts = [EA(nc.sbuf_tensor(f"hts{i}", [128, 128], F32)) for i in range(6)]
        pg = [EA(nc.psum_tensor(f"pgA{i}", [128, 512], F32)) for i in range(2)]
        pt = [EA(nc.psum_tensor(f"ptA{i}", [128, 128], F32)) for i in range(2)]
        pgat = EA(nc.psum_tensor("pgat", [128, 8], F32))

        v.wait_ge(s_t, p0_end[1])
        v.wait_ge(s_sp, p0_end[2])
        v.wait_ge(s_sc, p0_end[3])
        v.wait_ge(s_g, p0_g)
        V(v.memset(zeros_s[:, :], 0.0))
        ghalf_done = {}
        sigs = {}
        attn_v_mark = []
        htr_end = {}
        gat_tr_mark = {}
        cgl = {}
        hev_mark = []
        hsp_mark = []

        def emit_wload(k, half):
            if k == 0 and half == 0:
                sy.wait_ge(s_v, p0_end[0])
                sy.wait_ge(s_t, p0_end[1])
                sy.wait_ge(s_sp, p0_end[2])
                sy.wait_ge(s_sc, p0_end[3])
                sy.wait_ge(s_g, p0_g)
            if k > 0:
                sy.wait_ge(s_t, ghalf_done[(k - 1, half)])
            return LX(half, sy.dma_start(
                out=wbuf0[:, half * D:(half + 1) * D],
                in_=wint_d[16 * half + k]))

        def emit_gates_half(k, half, ldc):
            te.wait_ge(s_xp[half], ldc)
            lst = sigs.setdefault(k, [])
            for m in range(4 * half, 4 * half + 4):
                fslot, tcn = divmod(m, 4)
                if m >= 2:
                    te.wait_ge(s_sc, lst[m - 2])
                elif k > 0:
                    te.wait_ge(s_sc, sigs[k - 1][6 + m])
                last = None
                for e in range(NE):
                    last = te.matmul(pg[m % 2][:, :],
                                     wbuf0[:, fslot * D + 128 * e:
                                           fslot * D + 128 * (e + 1)],
                                     xnT[:, e * T + 512 * tcn:
                                         e * T + 512 * (tcn + 1)],
                                     start=(e == 0), stop=(e == NE - 1))
                gmk = TE(last)
                sc.wait_ge(s_t, gmk)
                if k > 1:
                    sc.wait_ge(s_v, attn_v_mark[k - 2])
                dst = (gx_b if fslot == 0 else ga_b)[
                    :, (k % 2) * T + 512 * tcn:(k % 2) * T + 512 * (tcn + 1)]
                lst.append(SC(sc.activation(dst, pg[m % 2][:, :],
                                            ACT.Sigmoid)))
            ghalf_done[(k, half)] = cnt["t"]

        def emit_chain(k):
            gx_t = gx_b[:, (k % 2) * T:(k % 2 + 1) * T]
            ga_t = ga_b[:, (k % 2) * T:(k % 2 + 1) * T]
            h_t = h_b[k % 2][:, :]
            ac_t = ac_s[:, :]
            v.wait_ge(s_sc, sigs[k][7])
            v.wait_ge(s_sc, c_afull_sc)
            if k >= 2:
                v.wait_ge(s_t, htr_end[k - 2])
            c_la = V(v.tensor_scalar(ga_t, ga_t,
                                     afull_s[:, k:k + 1], -C_GATE,
                                     AL.mult, AL.mult))
            sc.wait_ge(s_v, c_la)
            c_a = SC(sc.activation(ga_t, ga_t, ACT.Exp))
            v.wait_ge(s_sc, c_a)
            v.tensor_tensor(ac_t, ga_t, ga_t, AL.mult)
            v.tensor_scalar(ac_t, ac_t, -1.0, 1.0, AL.mult, AL.add)
            c_om = V(v.tensor_scalar(ac_t, ac_t, 0.0, None, AL.max))
            sc.wait_ge(s_v, c_om)
            c_sq = SC(sc.activation(ac_t, ac_t, ACT.Sqrt))
            v.wait_ge(s_sc, c_sq)
            v.tensor_tensor(ac_t, ac_t, gx_t, AL.mult)
            v.tensor_tensor(ac_t, ac_t, xnT[:, k * T:(k + 1) * T], AL.mult)
            v.tensor_tensor_scan(h_t, ga_t, ac_t, 0.0, AL.mult, AL.add)
            C4 = T // 4
            v.tensor_tensor_scan(ac_t[:, 0:C4], ga_t[:, 0:C4],
                                 zeros_s[:, :], 1.0, AL.mult, AL.add)
            for ch in range(1, 4):
                ci = V(v.tensor_tensor_scan(ac_t[:, ch * C4:(ch + 1) * C4],
                                            ga_t[:, ch * C4:(ch + 1) * C4],
                                            zeros_s[:, :],
                                            ac_t[:, ch * C4 - 1:ch * C4],
                                            AL.mult, AL.add))
            c_scan = ci
            attn_v_mark.append(cnt["v"])
            gp.wait_ge(s_v, c_scan)
            if k > 0:
                gp.wait_ge(s_t, gat_tr_mark[k - 1])
            c_snd = SP(gp.dma_start(out=csnd_d[k], in_=h_t[:, T - 1:T]))
            gp.wait_ge(s_sp, c_snd)
            gp.collective_compute(
                "AllGather", AL.bypass,
                replica_groups=[list(range(NCORES))],
                ins=[csnd_d[k]], outs=[cgat_d[k]],
            ).then_inc(s_cc, 1)
            cnt["cc"] += 1
            gp.wait_ge(s_cc, cnt["cc"])
            cgl[k] = SP(gp.dma_start(out=gat_s[:, :], in_=cgat_d[k]))

        def emit_tail(k):
            h_t = h_b[k % 2][:, :]
            ac_t = ac_s[:, :]
            te.wait_ge(s_sp, cgl[k])
            tm = TE(te.transpose(pgat[:, :], gat_s[0:8, 0:128],
                                 eye_s[0:8, 0:8]))
            gat_tr_mark[k] = tm
            v.wait_ge(s_t, tm)
            RT(v.tensor_copy(gm_s[:, :], pgat[:, :]))
            RT(v.tensor_tensor(gm_s[:, :], gm_s[:, :], cmask_s[:, :], AL.mult))
            RT(v.tensor_reduce(ceff_s[:, :], gm_s[:, :], AX.X, AL.add))
            c_fix = V(v.scalar_tensor_tensor(h_t, ac_t, ceff_s[:, 0:1], h_t,
                                             AL.mult, AL.add))
            te.wait_ge(s_v, c_fix)
            for j in range(NT):
                n = 16 * k + j
                if n >= 2:
                    te.wait_ge(s_sc, hev_mark[n - 2])
                tc = TE(te.transpose(pt[n % 2][:, :],
                                     h_t[:, 128 * j:128 * (j + 1)],
                                     eye_s[:, :]))
                sc.wait_ge(s_t, tc)
                if n >= 6:
                    sc.wait_ge(s_sp, hsp_mark[n - 6])
                em = SC(sc.activation(hts[n % 6][:, :], pt[n % 2][:, :],
                                      ACT.Identity))
                hev_mark.append(em)
                gp.wait_ge(s_sc, em)
                hsp_mark.append(
                    SP(gp.dma_start(out=xh_d[128 * j:128 * (j + 1),
                                            128 * k:128 * (k + 1)],
                                    in_=hts[n % 6][:, :])))
            htr_end[k] = cnt["t"]

        for k in range(NE):
            l0 = emit_wload(k, 0)
            l1 = emit_wload(k, 1)
            emit_gates_half(k, 0, l0)
            if k >= 1:
                emit_tail(k - 1)
            emit_gates_half(k, 1, l1)
            emit_chain(k)
        emit_tail(NE - 1)
        if DEBUG:
            gp.wait_ge(s_v, attn_v_mark[-1])
            SP(gp.dma_start(out=dbg_ga_d, in_=ga_b[:, T:2 * T]))
            SP(gp.dma_start(out=dbg_gx_d, in_=gx_b[:, T:2 * T]))
        a_end = (cnt["v"], cnt["t"], cnt["sp"], cnt["sc"])
        sA.close()
        s0a.close()

        sF = ExitStack()
        EF = sF.enter_context
        xnqh = EF(nc.sbuf_tensor("xnqh", [128, NE * 1024], BF16))
        oa = EF(nc.sbuf_tensor("oa", [128, 8 * D], F32))
        wgq = EF(nc.sbuf_tensor("wgq", [128, NE * 512], BF16))
        wuq = EF(nc.sbuf_tensor("wuq", [128, NE * 512], BF16))
        wdq = EF(nc.sbuf_tensor("wdq", [128, 4 * D], BF16))
        silu_t = EF(nc.sbuf_tensor("silu_t", [128, 2 * 512], F32))
        int_t = EF(nc.sbuf_tensor("int_t", [128, 2 * 512], F32))
        tq_t = EF(nc.sbuf_tensor("tq_t", [128, 2 * 512], BF16))
        tqT = EF(nc.sbuf_tensor("tqT", [128, 4 * 1024], BF16))
        xst = EF(nc.sbuf_tensor("xst", [128, D], F32))
        sgFb = EF(nc.sbuf_tensor("qF_sgB", [128, 1024], F32))
        qsF = {k: EF(nc.sbuf_tensor(f"qF_{k}", [128, w], F32))[:, :]
               for k, w in [("sg", 1024), ("av", 1024), ("m2", 1024),
                            ("m4", 1024), ("t0", 1024), ("t1", 1024),
                            ("t2", 1024), ("amax", 32),
                            ("gbsc", 256), ("lsc", 256), ("blk", 384)]}
        qsFb = dict(qsF)
        qsFb["sg"] = sgFb[:, :]
        pgf = [EF(nc.psum_tensor(f"pgf{i}", [128, 512], F32)) for i in range(2)]
        puf = [EF(nc.psum_tensor(f"puf{i}", [128, 512], F32)) for i in range(2)]
        pqt = [EF(nc.psum_tensor(f"pqt{i}", [128, 128], BF16)) for i in range(2)]
        pd = [EF(nc.psum_tensor(f"pd{i}", [128, 512], F32)) for i in range(2)]

        if DEBUG:
            gp.wait_ge(s_v, a_end[0])
            gp.wait_ge(s_sp, a_end[2])
            gp.wait_ge(s_cc, wq_cc)
            SP(gp.dma_start(out=dbg_h_d, in_=xh_d))
            SP(gp.dma_start(out=dbg_wg_d, in_=wgqt_d))
            SP(gp.dma_start(out=dbg_wd_d, in_=wdqt_d))
            SP(gp.dma_start(out=dbg_xnq_d, in_=xnqt_d))
        out_sp = []
        for H in HALVES:
            for e in range(NE):
                for jj in range(8):
                    if H == 0 and e == 0 and jj == 0:
                        sy.wait_ge(s_v, a_end[0])
                        sy.wait_ge(s_t, a_end[1])
                        sy.wait_ge(s_sp, a_end[2])
                        sy.wait_ge(s_sc, a_end[3])
                    if H == 1 and e == 0 and jj == 0:
                        sy.wait_ge(s_t, h0_t_end)
                    LD(sy.dma_start(
                        out=xnqh[:, e * 1024 + 128 * jj:e * 1024 + 128 * (jj + 1)],
                        in_=xnqt_d[e, 8 * H + jj]))
            oinit_mark = []
            if H == 1:
                sy.wait_ge(s_sp, out_sp[7])
            for j in range(8):
                r0 = 1024 * H + 128 * j
                LO(sy.dma_start(out=oa[:, D * j:D * (j + 1)],
                                in_=xh_d[r0:r0 + 128, :]))
            v.wait_ge(s_o, cnt["o"])
            for j in range(8):
                r0 = 1024 * H + 128 * j
                if j >= 1:
                    sy.wait_ge(s_v, oinit_mark[j - 1])
                px = LX(0, sy.dma_start(out=xst[:, :],
                                        in_=xs_d[r0:r0 + 128, :]))
                v.wait_ge(s_xp[0], px)
                oinit_mark.append(V(v.tensor_tensor(oa[:, D * j:D * (j + 1)],
                                                    oa[:, D * j:D * (j + 1)],
                                                    xst[:, :], AL.add)))
            dn_mark = {}
            if H == 0:
                slab_te_done = None
                gup7_prev = None
                gu_mark = {}
                pgf_free = {}
                puf_free = {}
                sbuf_free = {}
                tq_free = [None]
                pqt_free = {}
                pd_free = {}
                pqm_c = [None]
                tq_ctr = [0]
            for s in (SLABS if SLABS is not None else range(16)):
                r_rk, sl_lo = divmod(s, 2)
                if gup7_prev is not None:
                    sy.wait_ge(s_t, gup7_prev)
                elif H == 1:
                    sy.wait_ge(s_t, h0_t_end)
                if H == 0 and s == (SLABS[0] if SLABS is not None else 0):
                    sy.wait_ge(s_cc, wq_cc)
                LO(sy.dma_start(
                    out=wgq[:, :],
                    in_=wgqt_d[r_rk, :, sl_lo * 8192:(sl_lo + 1) * 8192]))
                gu_ld = LO(sy.dma_start(
                    out=wuq[:, :],
                    in_=wuqt_d[r_rk, :, sl_lo * 8192:(sl_lo + 1) * 8192]))
                if slab_te_done is not None:
                    sy.wait_ge(s_t, slab_te_done)
                for ii in range(4):
                    for ht in range(NE):
                        ldc = LD(sy.dma_start(
                            out=wdq[:, ii * D + 128 * ht:ii * D + 128 * (ht + 1)],
                            in_=wdqt_d[ht, 4 * s + ii]))
                wd_ld = ldc

                def emit_gup(j):
                    te.wait_ge(s_o, gu_ld)
                    bk = j % 2
                    if pgf_free.get(bk) is not None:
                        te.wait_ge(s_sc, pgf_free[bk])
                        te.wait_ge(s_v, puf_free[bk])
                    last_u = None
                    for e in range(NE):
                        stat = xnqh[:, e * 1024 + 128 * j:e * 1024 + 128 * (j + 1)]
                        te.matmul(pgf[bk][:, :], stat,
                                  wgq[:, e * 512:(e + 1) * 512],
                                  start=(e == 0), stop=(e == NE - 1))
                        last_u = te.matmul(puf[bk][:, :], stat,
                                           wuq[:, e * 512:(e + 1) * 512],
                                           start=(e == 0), stop=(e == NE - 1))
                    gu_mark[j] = TE(last_u)

                def emit_pre(j):
                    bk = j % 2
                    sl = silu_t[:, bk * 512:(bk + 1) * 512]
                    it = int_t[:, bk * 512:(bk + 1) * 512]
                    sc.wait_ge(s_t, gu_mark[j])
                    if sbuf_free.get(bk) is not None:
                        sc.wait_ge(s_v, sbuf_free[bk])
                    pgf_free[bk] = SC(sc.activation(sl, pgf[bk][:, :], ACT.Silu))
                    v.wait_ge(s_sc, pgf_free[bk])
                    im = V(v.tensor_tensor(it, sl, puf[bk][:, :], AL.mult))
                    puf_free[bk] = im
                    sbuf_free[bk] = im

                def emit_q(p):
                    if tq_free[0] is not None:
                        v.wait_ge(s_t, tq_free[0])
                    pqm_c[0] = quant(int_t[:, 0:1024], tq_t[:, 0:1024], 3,
                                     (qsF, qsFb)[p % 2], tie=True,
                                     x_guard=sbuf_free[1])

                def emit_tail_te(j):
                    bk = j % 2
                    tq = tq_t[:, bk * 512:(bk + 1) * 512]
                    te.wait_ge(*pqm_c[0])
                    ev_last = None
                    for ii in range(4):
                        n = tq_ctr[0]
                        tq_ctr[0] += 1
                        if pqt_free.get(n % 2) is not None:
                            te.wait_ge(s_sc, pqt_free[n % 2])
                        tc = TE(te.transpose(pqt[n % 2][:, :],
                                             tq[:, 128 * ii:128 * (ii + 1)],
                                             eyeb_s[:, :]))
                        if bk == 1 and ii == 3:
                            tq_free[0] = tc
                        sc.wait_ge(s_t, tc)
                        ev_last = SC(sc.activation(
                            tqT[:, ii * 1024 + 128 * j:ii * 1024 + 128 * (j + 1)],
                            pqt[n % 2][:, :], ACT.Identity))
                        pqt_free[n % 2] = ev_last
                    te.wait_ge(s_sc, ev_last)
                    te.wait_ge(s_ld, wd_ld)
                    for cgrp in range(4):
                        dbk = cgrp % 2
                        if pd_free.get(dbk) is not None:
                            te.wait_ge(s_v, pd_free[dbk])
                        last = None
                        for ii in range(4):
                            last = te.matmul(
                                pd[dbk][:, :],
                                tqT[:, ii * 1024 + 128 * j:ii * 1024 + 128 * (j + 1)],
                                wdq[:, ii * D + 512 * cgrp:ii * D + 512 * (cgrp + 1)],
                                start=(ii == 0), stop=(ii == 3))
                        dtc = TE(last)
                        v.wait_ge(s_t, dtc)
                        dn_mark[(j, cgrp)] = V(v.tensor_tensor(
                            oa[:, D * j + 512 * cgrp:D * j + 512 * (cgrp + 1)],
                            oa[:, D * j + 512 * cgrp:D * j + 512 * (cgrp + 1)],
                            pd[dbk][:, :], AL.add))
                        pd_free[dbk] = dn_mark[(j, cgrp)]

                emit_gup(0)
                emit_gup(1)
                for p in range(4):
                    emit_pre(2 * p)
                    if 2 * p + 2 < 8:
                        emit_gup(2 * p + 2)
                    emit_pre(2 * p + 1)
                    if 2 * p + 3 < 8:
                        emit_gup(2 * p + 3)
                    emit_q(p)
                    emit_tail_te(2 * p)
                    emit_tail_te(2 * p + 1)
                slab_te_done = cnt["t"]
                gup7_prev = gu_mark[7]
            if H == 0:
                h0_t_end = cnt["t"]
            out_sp = []
            for j in range(8):
                gp.wait_ge(s_v, dn_mark[(j, 3)])
                for cg in range(3):
                    gp.wait_ge(s_v, dn_mark[(j, cg)])
                r0 = 1024 * H + 128 * j
                out_sp.append(SP(gp.dma_start(out=out_d[r0:r0 + 128, :],
                                              in_=oa[:, D * j:D * (j + 1)])))
        if DEBUG:
            gp.wait_ge(s_v, cnt["v"])
            SP(gp.dma_start(out=dbg_q_d[:, 0:512], in_=qsF["sg"][:, 0:512]))
            SP(gp.dma_start(out=dbg_q_d[:, 512:1024], in_=qsF["av"][:, 0:512]))
            SP(gp.dma_start(out=dbg_q_d[:, 1024:1536], in_=qsF["t1"][:, 0:512]))
            SP(gp.dma_start(out=dbg_q_d[:, 1536:2048], in_=qsF["t0"][:, 0:512]))
            SP(gp.dma_start(out=dbg_q_d[:, 2048:2064], in_=qsF["amax"][:, 0:16]))
            SP(gp.dma_start(out=dbg_q_d[:, 2064:2576], in_=qsF["gbsc"]))
            SP(gp.dma_start(out=dbg_q_d[:, 2576:3344], in_=qsF["blk"]))
            SP(gp.dma_start(out=dbg_q_d[:, 3344:3856], in_=qsF["m4"][:, 0:512]))
            SP(gp.dma_start(out=dbg_int_d, in_=int_t[:, 512:1024]))
            SP(gp.dma_start(out=dbg_silu_d, in_=silu_t[:, 512:1024]))
            SP(gp.dma_start(out=dbg_tq_d, in_=tq_t[:, 512:1024]))
            SP(gp.dma_start(out=dbg_tqt2_d, in_=tqT[:, :]))
        sy.wait_ge(s_sp, cnt["sp"])
        gp.wait_ge(s_sp, cnt["sp"])
        sF.close()

    @blk.vector
    def _(v):
        body(v)

    perm.close()
    return nc


_NC_CACHE = [None]


def kernel(L0_attn_x, L0_attn_W_in, L0_attn_a_param,
           L0_ffn_w_gate, L0_ffn_w_up, L0_ffn_w_down):
    x = np.ascontiguousarray(np.asarray(L0_attn_x, dtype=np.float32))
    W_in = np.ascontiguousarray(np.asarray(L0_attn_W_in, dtype=np.float32))
    a_param = np.ascontiguousarray(np.asarray(L0_attn_a_param, dtype=np.float32))
    wg = np.ascontiguousarray(np.asarray(L0_ffn_w_gate, dtype=np.float32))
    wu = np.ascontiguousarray(np.asarray(L0_ffn_w_up, dtype=np.float32))
    wd = np.ascontiguousarray(np.asarray(L0_ffn_w_down, dtype=np.float32))

    WT = np.ascontiguousarray(W_in.T)
    wint = np.ascontiguousarray(
        WT.reshape(NE, 128, NF, 128).transpose(2, 1, 0, 3)
        .reshape(NF, 128, NE * 128))
    afull = np.ascontiguousarray(
        np.tile(a_param, NE).reshape(NE, 128).T)
    eye = np.eye(128, dtype=np.float32)
    cmasks = []
    for c in range(NCORES):
        m = np.zeros((128, 8), dtype=np.float32)
        if c % 2 == 1:
            m[:, c - 1] = 1.0
        cmasks.append(m)

    if _NC_CACHE[0] is None:
        _NC_CACHE[0] = build_program()
    nc = _NC_CACHE[0]

    in_maps = []
    for c in range(NCORES):
        b, half = divmod(c, 2)
        in_maps.append({
            "xs": np.ascontiguousarray(x[b, half * T:(half + 1) * T, :]),
            "wint": wint,
            "wg": np.ascontiguousarray(wg[c * (FI // 8):(c + 1) * (FI // 8)]),
            "wu": np.ascontiguousarray(wu[c * (FI // 8):(c + 1) * (FI // 8)]),
            "wd": np.ascontiguousarray(wd[c * (D // 8):(c + 1) * (D // 8)]),
            "afull": afull, "cmask": cmasks[c], "eye": eye,
        })
    res = run_bass_kernel_spmd(nc, in_maps, core_ids=list(range(NCORES)))
    out = np.empty((4, 2 * T, D), dtype=np.float32)
    for c in range(NCORES):
        b, half = divmod(c, 2)
        out[b, half * T:(half + 1) * T, :] = np.asarray(res.results[c]["out"])
    return out


# version 32
# speedup vs baseline: 1.0255x; 1.0255x over previous
import numpy as np
import concourse.bass as bass
import concourse.mybir as mybir
from concourse.bass_utils import run_bass_kernel_spmd

F32 = mybir.dt.float32
BF16 = mybir.dt.bfloat16
AL = mybir.AluOpType
ACT = mybir.ActivationFunctionType
AX = mybir.AxisListType

NCORES = 8
T = 2048
D = 2048
FI = 8192
NT = T // 128
NE = D // 128
NF = 32
NIF = FI // 128
C_GATE = 8.0
EPS = 1e-5
SCALE_LO = 2.0 ** -30
SCALE_HI = 2.0 ** 30


def _bc(ap_blk, inner):
    import dataclasses
    return dataclasses.replace(ap_blk, ap=list(ap_blk.ap) + [[0, inner]])


def build_program(DEBUG=False, SLABS=None, HALVES=(0, 1)):
    nc = bass.Bass("TRN2", target_bir_lowering=False,
                   detect_race_conditions=False)

    xs_d = nc.dram_tensor("xs", [T, D], F32, kind="ExternalInput").ap()
    wint_d = nc.dram_tensor("wint", [NF, 128, NE * 128], F32,
                            kind="ExternalInput").ap()
    wg_d = nc.dram_tensor("wg", [FI // 8, D], F32, kind="ExternalInput").ap()
    wu_d = nc.dram_tensor("wu", [FI // 8, D], F32, kind="ExternalInput").ap()
    wd_d = nc.dram_tensor("wd", [D // 8, FI], F32, kind="ExternalInput").ap()
    afull_d = nc.dram_tensor("afull", [128, 16], F32, kind="ExternalInput").ap()
    cmask_d = nc.dram_tensor("cmask", [128, 8], F32, kind="ExternalInput").ap()
    eye_d = nc.dram_tensor("eye", [128, 128], F32, kind="ExternalInput").ap()
    out_d = nc.dram_tensor("out", [T, D], F32, kind="ExternalOutput").ap()
    if DEBUG:
        dbg_h_d = nc.dram_tensor("dbg_h", [T, D], F32, kind="ExternalOutput").ap()
        dbg_wg_d = nc.dram_tensor("dbg_wg", [NIF, NE, 128, 128], BF16,
                                  kind="ExternalOutput").ap()
        dbg_xnq_d = nc.dram_tensor("dbg_xnq", [NE, NT, 128, 128], BF16,
                                   kind="ExternalOutput").ap()
        dbg_tqt_d = nc.dram_tensor("dbg_tqt", [128, 4096], BF16,
                                   kind="ExternalOutput").ap()
        dbg_ga_d = nc.dram_tensor("dbg_ga", [128, T], F32,
                                  kind="ExternalOutput").ap()
        dbg_gx_d = nc.dram_tensor("dbg_gx", [128, T], F32,
                                  kind="ExternalOutput").ap()
        dbg_xnt_d = nc.dram_tensor("dbg_xnt", [128, NE * T], F32,
                                   kind="ExternalOutput").ap()
        dbg_st_d = nc.dram_tensor("dbg_st", [128, 64], F32,
                                  kind="ExternalOutput").ap()
        dbg_int_d = nc.dram_tensor("dbg_int", [128, 512], F32,
                                   kind="ExternalOutput").ap()
        dbg_silu_d = nc.dram_tensor("dbg_silu", [128, 512], F32,
                                    kind="ExternalOutput").ap()
        dbg_tq_d = nc.dram_tensor("dbg_tq", [128, 512], BF16,
                                  kind="ExternalOutput").ap()
        dbg_tqt2_d = nc.dram_tensor("dbg_tqt2", [128, 4096], BF16,
                                    kind="ExternalOutput").ap()
        dbg_wd_d = nc.dram_tensor("dbg_wd", [NE, NIF, 128, 128], BF16,
                                  kind="ExternalOutput").ap()
        dbg_q_d = nc.dram_tensor("dbg_q", [128, 4096], F32,
                                 kind="ExternalOutput").ap()

    xh_d = nc.dram_tensor("xh_sp", [T, D], F32, kind="Internal").ap()
    xnqt_d = nc.dram_tensor("xnqt_sp", [NE, NT, 128, 128], BF16,
                            kind="Internal").ap()
    wgqt_d = nc.dram_tensor("wgqt_sp", [8, 128, 2 * 8192], BF16,
                            kind="Internal", addr_space="Shared").ap()
    wuqt_d = nc.dram_tensor("wuqt_sp", [8, 128, 2 * 8192], BF16,
                            kind="Internal", addr_space="Shared").ap()
    wdqt_d = nc.dram_tensor("wdqt_sp", [NE, NIF, 128, 128], BF16,
                            kind="Internal", addr_space="Shared").ap()
    wgsh_d = nc.dram_tensor("wgsh_sp", [128, 2 * 8192], BF16,
                            kind="Internal").ap()
    wush_d = nc.dram_tensor("wush_sp", [128, 2 * 8192], BF16,
                            kind="Internal").ap()
    wdsh_d = nc.dram_tensor("wdsh_sp", [NE // 8, NIF, 128, 128], BF16,
                            kind="Internal").ap()
    csnd_d = [nc.dram_tensor(f"csnd{k}", [1, 128], F32, kind="Internal").ap()
              for k in range(NE)]
    cgat_d = [nc.dram_tensor(f"cgat{k}", [8, 128], F32, kind="Internal",
                          addr_space="Shared").ap()
              for k in range(NE)]

    from contextlib import ExitStack
    perm = ExitStack()
    E = perm.enter_context

    eye_s = E(nc.sbuf_tensor("eye_s", [128, 128], F32))
    eyeb_s = E(nc.sbuf_tensor("eyeb_s", [128, 128], BF16))
    afull_s = E(nc.sbuf_tensor("afull_s", [128, 16], F32))
    cmask_s = E(nc.sbuf_tensor("cmask_s", [128, 8], F32))
    epsb_s = E(nc.sbuf_tensor("epsb_s", [128, 1], F32))

    s_ld = E(nc.semaphore("s_ld"))
    s_sp = E(nc.semaphore("s_sp"))
    s_v = E(nc.semaphore("s_v"))
    s_sc = E(nc.semaphore("s_sc"))
    s_t = E(nc.semaphore("s_t"))
    s_cc = E(nc.semaphore("s_cc"))
    s_rt = E(nc.semaphore("s_rt"))
    s_g = E(nc.semaphore("s_g"))
    s_o = E(nc.semaphore("s_o"))
    s_x0 = E(nc.semaphore("s_x0"))
    s_x1 = E(nc.semaphore("s_x1"))

    blk = E(nc.Block())
    cnt = {"ld": 0, "sp": 0, "v": 0, "sc": 0, "t": 0, "cc": 0, "rt": 0,
           "g": 0, "o": 0, "x0": 0, "x1": 0}

    def body(v):
        sy, sc, te, gp = nc.sync, nc.scalar, nc.tensor, nc.gpsimd

        def LD(inst):
            inst.then_inc(s_ld, 16); cnt["ld"] += 16; return cnt["ld"]

        def SP(inst):
            inst.then_inc(s_sp, 16); cnt["sp"] += 16; return cnt["sp"]

        def V(inst):
            inst.then_inc(s_v, 1); cnt["v"] += 1; return cnt["v"]

        def SC(inst):
            inst.then_inc(s_sc, 1); cnt["sc"] += 1; return cnt["sc"]

        def TE(inst):
            inst.then_inc(s_t, 1); cnt["t"] += 1; return cnt["t"]

        def GP(inst):
            inst.then_inc(s_g, 1); cnt["g"] += 1; return cnt["g"]

        s_xp = (s_x0, s_x1)

        def LO(inst):
            inst.then_inc(s_o, 16); cnt["o"] += 16; return cnt["o"]

        def LX(p, inst):
            inst.then_inc(s_xp[p], 16); cnt["x%d" % p] += 16
            return cnt["x%d" % p]

        def RT(inst):
            inst.then_inc(s_rt, 1); cnt["rt"] += 1
            v.wait_ge(s_rt, cnt["rt"])

        def scale_ladder(amax_ap, nb, rep, gb_t, sc_t, rb_ap, mfrac, q):
            W = rep * nb
            va = sc_t[:, 0:W]
            fac = sc_t[:, W:2 * W]
            cb = gb_t[:, W:2 * W]
            gb = gb_t[:, 0:nb]
            rb = rb_ap[:, 0:W]
            if rep > 1:
                va3 = va.rearrange("p (r n) -> p r n", r=rep)
                src = _bc(amax_ap, rep).rearrange("p n r -> p r n")
            else:
                va3 = va
                src = amax_ap
            q.memset(rb, float(2.0 ** (30 + 2 + mfrac)))
            q.tensor_scalar(va3, src, SCALE_LO, SCALE_HI, AL.max, AL.min)
            q.tensor_scalar(va, va, 2.0 ** 30, None, AL.mult)
            for j in (16, 16, 16, 8, 4, 2, 1):
                q.tensor_scalar(cb, va, float(2.0 ** j), None, AL.is_ge)
                q.tensor_scalar(fac, cb, float(2.0 ** (-j) - 1.0), 1.0,
                                AL.mult, AL.add)
                q.tensor_tensor(va, va, fac, AL.mult)
                q.tensor_tensor(rb, rb, fac, AL.mult)
            cnt["rt"] += 1
            v.reciprocal(gb, rb[:, 0:nb]).then_inc(s_rt, 1)
            return gb, W, cnt["rt"]

        def quant(x_ap, q_ap, mfrac, qs, tie, eng="v", x_guard=None):
            F = x_ap.shape[-1]
            nb = F // 32
            q = v if eng == "v" else gp
            QM = V if eng == "v" else GP
            qsem = s_v if eng == "v" else s_g
            sg = qs["sg"][:, 0:F]; av = qs["av"][:, 0:F]
            m2 = qs["m2"][:, 0:F]; m4 = qs["m4"][:, 0:F]
            t0 = qs["t0"][:, 0:F]; t1 = qs["t1"][:, 0:F]
            amax = qs["amax"][:, 0:nb]
            cap = float(2.0 ** (mfrac + 1))
            if x_guard is None:
                sc.wait_ge(s_v, cnt["v"])
            elif x_guard:
                sc.wait_ge(s_v, x_guard)
            sgp = qs.get("_sgp")
            if sgp:
                sc.wait_ge(s_v, sgp)
            prev = qs.get("_last")
            if prev is not None and prev[0] is s_g:
                sc.wait_ge(s_g, prev[1])
                v.wait_ge(s_g, prev[1])
            sgn_c = SC(sc.activation(sg, x_ap, ACT.Sign))
            rep = 1 if eng == "gp" else max(1, (128 + nb - 1) // nb)
            W = rep * nb
            rbW = qs["blk"][:, 2 * W:3 * W]
            x3 = x_ap.rearrange("p (b k) -> p b k", k=32)
            cnt["rt"] += 1
            v.tensor_reduce(amax, x3, AX.X, AL.max,
                            apply_absolute_value=True).then_inc(s_rt, 1)
            q.wait_ge(s_rt, cnt["rt"])
            gb, W, rt_gb = scale_ladder(amax, nb, rep, qs["gbsc"],
                                        qs["lsc"], rbW, mfrac, q)
            rb = rbW[:, 0:nb]
            q.wait_ge(s_sc, sgn_c)
            q.tensor_tensor(av, x_ap, sg, AL.mult)
            q.tensor_tensor(av, av, _bc(rb, 32), AL.mult)
            q.tensor_scalar(m2, av, cap, None, AL.is_ge)
            q.tensor_scalar(m4, av, 2.0 * cap, None, AL.is_ge)
            q.tensor_scalar(t0, m2, -0.5, 1.0, AL.mult, AL.add)
            q.tensor_scalar(t1, m4, -0.5, 1.0, AL.mult, AL.add)
            q.tensor_tensor(t0, t0, t1, AL.mult)
            q.tensor_tensor(t1, av, t0, AL.mult)
            q.tensor_copy(av, t1)
            for j in [2 ** b for b in range(mfrac, -1, -1)]:
                q.tensor_scalar(t0, av, float(j), None, AL.is_ge)
                if eng == "v":
                    q.scalar_tensor_tensor(av, t0, float(-j), av,
                                           AL.mult, AL.add)
                elif j > 1:
                    q.tensor_scalar(t0, t0, float(j), None, AL.mult)
                    q.tensor_tensor(av, av, t0, AL.subtract)
                else:
                    q.tensor_tensor(av, av, t0, AL.subtract)
            q.tensor_tensor(t1, t1, av, AL.subtract)
            if tie:
                t2 = qs["t2"][:, 0:F]
                q.tensor_scalar(t2, av, 0.5, None, AL.is_equal)
                q.tensor_tensor(t2, t2, t0, AL.mult)
                q.tensor_scalar(av, av, 0.5, None, AL.is_gt)
                q.tensor_tensor(t1, t1, av, AL.add)
                q.tensor_tensor(t1, t1, t2, AL.add)
            else:
                q.tensor_scalar(av, av, 0.5, None, AL.is_gt)
                q.tensor_tensor(t1, t1, av, AL.add)
            q.tensor_scalar(t1, t1, cap, None, AL.min)
            q.tensor_scalar(av, t1, cap, None, AL.is_equal)
            q.tensor_tensor(t0, av, m4, AL.mult)
            q.tensor_tensor(t1, t1, t0, AL.subtract)
            q.tensor_scalar(m2, m2, 1.0, None, AL.add)
            q.tensor_scalar(m4, m4, 1.0, None, AL.add)
            q.tensor_tensor(m2, m2, m4, AL.mult)
            q.wait_ge(s_rt, rt_gb)
            q.tensor_tensor(m2, m2, _bc(gb, 32), AL.mult)
            q.tensor_tensor(t1, t1, m2, AL.mult)
            mk = QM(q.tensor_tensor(q_ap, t1, sg, AL.mult))
            qs["_last"] = (qsem, mk)
            if eng == "v":
                qs["_sgp"] = mk
            return qsem, mk

        LD(sy.dma_start(out=eye_s[:, :], in_=eye_d))
        LD(sy.dma_start(out=afull_s[:, :], in_=afull_d))
        c_consts = LD(sy.dma_start(out=cmask_s[:, :], in_=cmask_d))
        sc.wait_ge(s_ld, c_consts)
        sp1 = SC(sc.activation(afull_s[:, :], afull_s[:, :], ACT.Exp))
        sc.wait_ge(s_sc, sp1)
        SC(sc.activation(afull_s[:, :], afull_s[:, :], ACT.Ln, bias=1.0))
        c_afull_sc = cnt["sc"]
        v.wait_ge(s_ld, c_consts)
        RT(v.memset(epsb_s[:, :], EPS))
        V(v.tensor_copy(eyeb_s[:, :], eye_s[:, :]))
        c_eyeb_v = cnt["v"]

        wsc = ExitStack()
        EW = wsc.enter_context
        wst = [EW(nc.sbuf_tensor(f"wst{i}", [128, D], F32)) for i in range(2)]
        wq_s = EW(nc.sbuf_tensor("wq_s", [128, D], BF16))
        qsW = {k: EW(nc.sbuf_tensor(f"qw_{k}", [128, w], F32))[:, :]
               for k, w in [("sg", D), ("av", D), ("m2", D), ("m4", D),
                            ("t0", D), ("t1", D), ("t2", D), ("amax", 64),
                            ("gbsc", 512), ("lsc", 512), ("blk", 768)]}
        tev = [EW(nc.sbuf_tensor(f"tev{i}", [128, 128], BF16)) for i in range(8)]
        sgWb = EW(nc.sbuf_tensor("qw_sgB", [128, D], F32))
        qsW2 = dict(qsW)
        qsW2["sg"] = sgWb[:, :]
        pw = [EW(nc.psum_tensor(f"pw{i}", [128, 128], BF16)) for i in range(2)]

        wjobs = []

        def _gu_dst(sh_d, rl, e):
            c0 = (rl // 4) * 8192 + e * 512 + (rl % 4) * 128
            return sh_d[:, c0:c0 + 128]

        for rl in range(NIF // 8):
            wjobs.append((wg_d[128 * rl:128 * (rl + 1), :],
                          [_gu_dst(wgsh_d, rl, e) for e in range(NE)]))
        for rl in range(NIF // 8):
            wjobs.append((wu_d[128 * rl:128 * (rl + 1), :],
                          [_gu_dst(wush_d, rl, e) for e in range(NE)]))
        for hl in range(D // 128 // 8):
            for s in range(FI // D):
                wjobs.append((wd_d[128 * hl:128 * (hl + 1), D * s:D * (s + 1)],
                              [wdsh_d[hl, s * 16 + e] for e in range(NE)]))

        cons_mark = []
        ev_mark = []
        sp_mark = []
        tr_mark = []
        for i, (src, dsts) in enumerate(wjobs):
            if i >= 2:
                sy.wait_ge(*cons_mark[i - 2])
            ldc = LX(i % 2, sy.dma_start(out=wst[i % 2][:, :], in_=src))
            sc.wait_ge(s_xp[i % 2], ldc)
            v.wait_ge(s_xp[i % 2], ldc)
            if i >= 1:
                v.wait_ge(s_t, tr_mark[i - 1])
            qsem, qlast = quant(wst[i % 2][:, :], wq_s[:, :], 1,
                                (qsW, qsW2)[i % 2], tie=True, x_guard=0)
            cons_mark.append((qsem, qlast))
            te.wait_ge(qsem, qlast)
            for e in range(NE):
                n = 16 * i + e
                if n >= 2:
                    te.wait_ge(s_sc, ev_mark[n - 2])
                tc = TE(te.transpose(pw[n % 2][:, :],
                                     wq_s[:, 128 * e:128 * (e + 1)],
                                     eyeb_s[:, :]))
                sc.wait_ge(s_t, tc)
                if n >= 8:
                    sc.wait_ge(s_sp, sp_mark[n - 8])
                em = SC(sc.activation(tev[n % 8][:, :], pw[n % 2][:, :],
                                      ACT.Identity))
                ev_mark.append(em)
                gp.wait_ge(s_sc, em)
                sp_mark.append(SP(gp.dma_start(out=dsts[e],
                                               in_=tev[n % 8][:, :])))
            tr_mark.append(cnt["t"])
        w_end_v, w_end_t, w_end_sp = cnt["v"], cnt["t"], cnt["sp"]
        w_end_g = cnt["g"]
        w_end_sc = cnt["sc"]
        wsc.close()

        s0a = ExitStack()
        E0A = s0a.enter_context
        xnT = E0A(nc.sbuf_tensor("xnT", [128, NE * T], F32))

        s0 = ExitStack()
        E0 = s0.enter_context
        x_t = [E0(nc.sbuf_tensor(f"x_t{i}", [128, D], F32)) for i in range(2)]
        xn_t = E0(nc.sbuf_tensor("xn_t", [128, D], F32))
        xnq_t = [E0(nc.sbuf_tensor(f"xnq_t{i}", [128, D], BF16))
                 for i in range(2)]
        sums = E0(nc.sbuf_tensor("sums", [128, 16], F32))
        sqs = E0(nc.sbuf_tensor("sqs", [128, 16], F32))
        rstd = E0(nc.sbuf_tensor("rstd", [128, 16], F32))
        nbias = E0(nc.sbuf_tensor("nbias", [128, 16], F32))
        stt16 = E0(nc.sbuf_tensor("stt16", [128, 16], F32))
        sqscr = E0(nc.sbuf_tensor("sqscr", [128, D // 2], F32))
        qs0 = {k: E0(nc.sbuf_tensor(f"q0_{k}", [128, w], F32))[:, :]
               for k, w in [("sg", 1024), ("av", 1024), ("m2", 1024),
                            ("m4", 1024), ("t0", 1024), ("t1", 1024), ("t2", 1024),
                            ("amax", 32), ("gbsc", 512), ("lsc", 512),
                            ("blk", 768)]}
        sg0b = E0(nc.sbuf_tensor("q0_sgB", [128, 1024], F32))
        qs0b = dict(qs0)
        qs0b["sg"] = sg0b[:, :]
        p0f = [E0(nc.psum_tensor(f"p0f{i}", [128, 128], F32)) for i in range(2)]
        p0b = [E0(nc.psum_tensor(f"p0b{i}", [128, 128], BF16)) for i in range(2)]
        ev0 = [E0(nc.sbuf_tensor(f"ev0{i}", [128, 128], BF16)) for i in range(8)]

        st_mark = []
        for j in range(NT):
            if j < 2:
                sy.wait_ge(s_v, w_end_v)
                sy.wait_ge(s_t, w_end_t)
                sy.wait_ge(s_sp, w_end_sp)
                sy.wait_ge(s_g, w_end_g)
                sy.wait_ge(s_sc, w_end_sc)
            else:
                sy.wait_ge(s_v, st_mark[j - 2])
            ldc = LX(j % 2, sy.dma_start(out=x_t[j % 2][:, :],
                                         in_=xs_d[128 * j:128 * (j + 1), :]))
            v.wait_ge(s_xp[j % 2], ldc)
            v.tensor_reduce(sums[:, j:j + 1], x_t[j % 2][:, :], AX.X, AL.add)
            v.tensor_tensor(sqscr[:, :], x_t[j % 2][:, 0:D // 2],
                            x_t[j % 2][:, 0:D // 2], AL.mult)
            v.tensor_reduce(sqs[:, j:j + 1], sqscr[:, :], AX.X, AL.add)
            v.tensor_tensor(sqscr[:, :], x_t[j % 2][:, D // 2:D],
                            x_t[j % 2][:, D // 2:D], AL.mult)
            st_mark.append(V(v.tensor_reduce(stt16[:, j:j + 1], sqscr[:, :],
                                             AX.X, AL.add)))
        v.wait_ge(s_v, st_mark[-1])
        RT(v.tensor_tensor(sqs[:, :], sqs[:, :], stt16[:, :], AL.add))
        RT(v.tensor_scalar(sums[:, :], sums[:, :], 1.0 / D, None, AL.mult))
        RT(v.tensor_scalar(sqs[:, :], sqs[:, :], 1.0 / D, None, AL.mult))
        RT(v.tensor_tensor(stt16[:, :], sums[:, :], sums[:, :], AL.mult))
        c_var = V(v.tensor_tensor(sqs[:, :], sqs[:, :], stt16[:, :],
                                  AL.subtract))
        sc.wait_ge(s_v, c_var)
        c_sqv = SC(sc.activation(stt16[:, :], sqs[:, :], ACT.Sqrt,
                                 bias=epsb_s[:, 0:1]))
        v.wait_ge(s_sc, c_sqv)
        RT(v.reciprocal(rstd[:, :], stt16[:, :]))
        RT(v.scalar_tensor_tensor(nbias[:, :], sums[:, :], -1.0, rstd[:, :],
                                  AL.mult, AL.mult))
        c_stats_rt = cnt["rt"]

        evf_mark = []
        evb_mark = []
        spq_mark = []
        xn_tr_mark = []
        bf_mark = []
        pq_mark = []
        cxn_mark = []

        def emit_bf16(jj):
            te.wait_ge(*pq_mark[jj])
            for e in range(NE):
                n = 16 * jj + e
                if n >= 2:
                    te.wait_ge(s_sc, evb_mark[n - 2])
                tc = TE(te.transpose(p0b[n % 2][:, :],
                                     xnq_t[jj % 2][:, 128 * e:128 * (e + 1)],
                                     eyeb_s[:, :]))
                sc.wait_ge(s_t, tc)
                if n >= 8:
                    sc.wait_ge(s_sp, spq_mark[n - 8])
                em = SC(sc.activation(ev0[n % 8][:, :], p0b[n % 2][:, :],
                                      ACT.Identity))
                evb_mark.append(em)
                gp.wait_ge(s_sc, em)
                spq_mark.append(SP(gp.dma_start(out=xnqt_d[e, jj],
                                                in_=ev0[n % 8][:, :])))
            bf_mark.append(cnt["t"])

        for j in range(NT):
            if j == 0:
                sy.wait_ge(s_v, st_mark[-1])
            if j >= 2:
                sy.wait_ge(s_sc, cxn_mark[j - 2])
            ldc = LX(j % 2, sy.dma_start(out=x_t[j % 2][:, :],
                                         in_=xs_d[128 * j:128 * (j + 1), :]))
            sc.wait_ge(s_xp[j % 2], ldc)
            sc.wait_ge(s_rt, c_stats_rt)
            if j > 0:
                sc.wait_ge(s_t, xn_tr_mark[j - 1])
                sc.wait_ge(*pq_mark[j - 1])
            c_xn = SC(sc.activation(xn_t[:, :], x_t[j % 2][:, :], ACT.Identity,
                                    bias=nbias[:, j:j + 1],
                                    scale=rstd[:, j:j + 1]))
            cxn_mark.append(c_xn)
            if j > 0:
                emit_bf16(j - 1)
            v.wait_ge(s_sc, c_xn)
            if j >= 2:
                v.wait_ge(s_t, bf_mark[j - 2])
            quant(xn_t[:, 0:1024], xnq_t[j % 2][:, 0:1024], 3, qs0,
                  tie=True, x_guard=0)
            pq = quant(xn_t[:, 1024:2048], xnq_t[j % 2][:, 1024:2048], 3,
                       qs0b, tie=True, x_guard=0)
            pq_mark.append(pq)
            te.wait_ge(s_sc, c_xn)
            for e in range(NE):
                n = 16 * j + e
                if n >= 2:
                    te.wait_ge(s_sc, evf_mark[n - 2])
                tc = TE(te.transpose(p0f[n % 2][:, :],
                                     xn_t[:, 128 * e:128 * (e + 1)],
                                     eye_s[:, :]))
                sc.wait_ge(s_t, tc)
                evf_mark.append(SC(sc.activation(
                    xnT[:, e * T + 128 * j:e * T + 128 * (j + 1)],
                    p0f[n % 2][:, :], ACT.Identity)))
            xn_tr_mark.append(cnt["t"])
        emit_bf16(NT - 1)
        p0_end = (cnt["v"], cnt["t"], cnt["sp"], cnt["sc"])
        p0_g = cnt["g"]
        gp.wait_ge(s_sp, w_end_sp)
        for src_t, dst_t in ((wgsh_d, wgqt_d), (wush_d, wuqt_d),
                             (wdsh_d, wdqt_d)):
            gp.collective_compute(
                "AllGather", AL.bypass,
                replica_groups=[list(range(NCORES))],
                ins=[src_t], outs=[dst_t],
            ).then_inc(s_cc, 1)
            cnt["cc"] += 1
        wq_cc = cnt["cc"]
        if DEBUG:
            gp.wait_ge(s_v, p0_end[0])
            gp.wait_ge(s_sc, p0_end[3])
            SP(gp.dma_start(out=dbg_xnt_d, in_=xnT[:, :]))
            SP(gp.dma_start(out=dbg_st_d[:, 0:16], in_=sums[:, :]))
            SP(gp.dma_start(out=dbg_st_d[:, 16:32], in_=sqs[:, :]))
            SP(gp.dma_start(out=dbg_st_d[:, 32:48], in_=rstd[:, :]))
            SP(gp.dma_start(out=dbg_st_d[:, 48:64], in_=nbias[:, :]))
        s0.close()

        sA = ExitStack()
        EA = sA.enter_context
        wbuf0 = EA(nc.sbuf_tensor("wbuf0", [128, 2 * D], F32))
        zeros_s = EA(nc.sbuf_tensor("zeros_s", [128, T // 4], F32))
        gx_b = EA(nc.sbuf_tensor("gx_b", [128, 2 * T], F32))
        ga_b = EA(nc.sbuf_tensor("ga_b", [128, 2 * T], F32))
        h_b = [EA(nc.sbuf_tensor(f"h_b{i}", [128, T], F32)) for i in range(2)]
        ac_s = EA(nc.sbuf_tensor("ac_s", [128, T], F32))
        gat_s = EA(nc.sbuf_tensor("gat_s", [8, 128], F32))
        gm_s = EA(nc.sbuf_tensor("gm_s", [128, 8], F32))
        ceff_s = EA(nc.sbuf_tensor("ceff_s", [128, 1], F32))
        hts = [EA(nc.sbuf_tensor(f"hts{i}", [128, 128], F32)) for i in range(6)]
        pg = [EA(nc.psum_tensor(f"pgA{i}", [128, 512], F32)) for i in range(2)]
        pt = [EA(nc.psum_tensor(f"ptA{i}", [128, 128], F32)) for i in range(2)]
        pgat = EA(nc.psum_tensor("pgat", [128, 8], F32))

        v.wait_ge(s_t, p0_end[1])
        v.wait_ge(s_sp, p0_end[2])
        v.wait_ge(s_sc, p0_end[3])
        v.wait_ge(s_g, p0_g)
        V(v.memset(zeros_s[:, :], 0.0))
        ghalf_done = {}
        sigs = {}
        attn_v_mark = []
        htr_end = {}
        gat_tr_mark = {}
        cgl = {}
        hev_mark = []
        hsp_mark = []

        def emit_wload(k, half):
            if k == 0 and half == 0:
                sy.wait_ge(s_v, p0_end[0])
                sy.wait_ge(s_t, p0_end[1])
                sy.wait_ge(s_sp, p0_end[2])
                sy.wait_ge(s_sc, p0_end[3])
                sy.wait_ge(s_g, p0_g)
            if k > 0:
                sy.wait_ge(s_t, ghalf_done[(k - 1, half)])
            return LX(half, sy.dma_start(
                out=wbuf0[:, half * D:(half + 1) * D],
                in_=wint_d[16 * half + k]))

        def emit_gates_half(k, half, ldc):
            te.wait_ge(s_xp[half], ldc)
            lst = sigs.setdefault(k, [])
            for m in range(4 * half, 4 * half + 4):
                fslot, tcn = divmod(m, 4)
                if m >= 2:
                    te.wait_ge(s_sc, lst[m - 2])
                elif k > 0:
                    te.wait_ge(s_sc, sigs[k - 1][6 + m])
                last = None
                for e in range(NE):
                    last = te.matmul(pg[m % 2][:, :],
                                     wbuf0[:, fslot * D + 128 * e:
                                           fslot * D + 128 * (e + 1)],
                                     xnT[:, e * T + 512 * tcn:
                                         e * T + 512 * (tcn + 1)],
                                     start=(e == 0), stop=(e == NE - 1))
                gmk = TE(last)
                sc.wait_ge(s_t, gmk)
                if k > 1:
                    sc.wait_ge(s_v, attn_v_mark[k - 2])
                dst = (gx_b if fslot == 0 else ga_b)[
                    :, (k % 2) * T + 512 * tcn:(k % 2) * T + 512 * (tcn + 1)]
                lst.append(SC(sc.activation(dst, pg[m % 2][:, :],
                                            ACT.Sigmoid)))
            ghalf_done[(k, half)] = cnt["t"]

        def emit_chain(k):
            gx_t = gx_b[:, (k % 2) * T:(k % 2 + 1) * T]
            ga_t = ga_b[:, (k % 2) * T:(k % 2 + 1) * T]
            h_t = h_b[k % 2][:, :]
            ac_t = ac_s[:, :]
            v.wait_ge(s_sc, sigs[k][7])
            v.wait_ge(s_sc, c_afull_sc)
            if k >= 2:
                v.wait_ge(s_t, htr_end[k - 2])
            c_la = V(v.tensor_scalar(ga_t, ga_t,
                                     afull_s[:, k:k + 1], -C_GATE,
                                     AL.mult, AL.mult))
            sc.wait_ge(s_v, c_la)
            c_a = SC(sc.activation(ga_t, ga_t, ACT.Exp))
            v.wait_ge(s_sc, c_a)
            v.tensor_tensor(ac_t, ga_t, ga_t, AL.mult)
            v.tensor_scalar(ac_t, ac_t, -1.0, 1.0, AL.mult, AL.add)
            c_om = V(v.tensor_scalar(ac_t, ac_t, 0.0, None, AL.max))
            sc.wait_ge(s_v, c_om)
            c_sq = SC(sc.activation(ac_t, ac_t, ACT.Sqrt))
            v.wait_ge(s_sc, c_sq)
            v.tensor_tensor(ac_t, ac_t, gx_t, AL.mult)
            v.tensor_tensor(ac_t, ac_t, xnT[:, k * T:(k + 1) * T], AL.mult)
            v.tensor_tensor_scan(h_t, ga_t, ac_t, 0.0, AL.mult, AL.add)
            C4 = T // 4
            v.tensor_tensor_scan(ac_t[:, 0:C4], ga_t[:, 0:C4],
                                 zeros_s[:, :], 1.0, AL.mult, AL.add)
            for ch in range(1, 4):
                ci = V(v.tensor_tensor_scan(ac_t[:, ch * C4:(ch + 1) * C4],
                                            ga_t[:, ch * C4:(ch + 1) * C4],
                                            zeros_s[:, :],
                                            ac_t[:, ch * C4 - 1:ch * C4],
                                            AL.mult, AL.add))
            c_scan = ci
            attn_v_mark.append(cnt["v"])
            gp.wait_ge(s_v, c_scan)
            if k > 0:
                gp.wait_ge(s_t, gat_tr_mark[k - 1])
            c_snd = SP(gp.dma_start(out=csnd_d[k], in_=h_t[:, T - 1:T]))
            gp.wait_ge(s_sp, c_snd)
            gp.collective_compute(
                "AllGather", AL.bypass,
                replica_groups=[list(range(NCORES))],
                ins=[csnd_d[k]], outs=[cgat_d[k]],
            ).then_inc(s_cc, 1)
            cnt["cc"] += 1
            gp.wait_ge(s_cc, cnt["cc"])
            cgl[k] = SP(gp.dma_start(out=gat_s[:, :], in_=cgat_d[k]))

        def emit_tail(k):
            h_t = h_b[k % 2][:, :]
            ac_t = ac_s[:, :]
            te.wait_ge(s_sp, cgl[k])
            tm = TE(te.transpose(pgat[:, :], gat_s[0:8, 0:128],
                                 eye_s[0:8, 0:8]))
            gat_tr_mark[k] = tm
            v.wait_ge(s_t, tm)
            RT(v.tensor_copy(gm_s[:, :], pgat[:, :]))
            RT(v.tensor_tensor(gm_s[:, :], gm_s[:, :], cmask_s[:, :], AL.mult))
            RT(v.tensor_reduce(ceff_s[:, :], gm_s[:, :], AX.X, AL.add))
            c_fix = V(v.scalar_tensor_tensor(h_t, ac_t, ceff_s[:, 0:1], h_t,
                                             AL.mult, AL.add))
            te.wait_ge(s_v, c_fix)
            for j in range(NT):
                n = 16 * k + j
                if n >= 2:
                    te.wait_ge(s_sc, hev_mark[n - 2])
                tc = TE(te.transpose(pt[n % 2][:, :],
                                     h_t[:, 128 * j:128 * (j + 1)],
                                     eye_s[:, :]))
                sc.wait_ge(s_t, tc)
                if n >= 6:
                    sc.wait_ge(s_sp, hsp_mark[n - 6])
                em = SC(sc.activation(hts[n % 6][:, :], pt[n % 2][:, :],
                                      ACT.Identity))
                hev_mark.append(em)
                gp.wait_ge(s_sc, em)
                hsp_mark.append(
                    SP(gp.dma_start(out=xh_d[128 * j:128 * (j + 1),
                                            128 * k:128 * (k + 1)],
                                    in_=hts[n % 6][:, :])))
            htr_end[k] = cnt["t"]

        for k in range(NE):
            l0 = emit_wload(k, 0)
            l1 = emit_wload(k, 1)
            emit_gates_half(k, 0, l0)
            if k >= 1:
                emit_tail(k - 1)
            emit_gates_half(k, 1, l1)
            emit_chain(k)
        emit_tail(NE - 1)
        if DEBUG:
            gp.wait_ge(s_v, attn_v_mark[-1])
            SP(gp.dma_start(out=dbg_ga_d, in_=ga_b[:, T:2 * T]))
            SP(gp.dma_start(out=dbg_gx_d, in_=gx_b[:, T:2 * T]))
        a_end = (cnt["v"], cnt["t"], cnt["sp"], cnt["sc"])
        sA.close()
        s0a.close()

        sF = ExitStack()
        EF = sF.enter_context
        xnqh = EF(nc.sbuf_tensor("xnqh", [128, NE * 1024], BF16))
        oa = EF(nc.sbuf_tensor("oa", [128, 8 * D], F32))
        wgq = EF(nc.sbuf_tensor("wgq", [128, NE * 512], BF16))
        wuq = EF(nc.sbuf_tensor("wuq", [128, NE * 512], BF16))
        wdq = EF(nc.sbuf_tensor("wdq", [128, 4 * D], BF16))
        silu_t = EF(nc.sbuf_tensor("silu_t", [128, 2 * 512], F32))
        int_t = EF(nc.sbuf_tensor("int_t", [128, 2 * 512], F32))
        tq_t = EF(nc.sbuf_tensor("tq_t", [128, 2 * 512], BF16))
        tqT = EF(nc.sbuf_tensor("tqT", [128, 4 * 1024], BF16))
        xst = EF(nc.sbuf_tensor("xst", [128, D], F32))
        sgFb = EF(nc.sbuf_tensor("qF_sgB", [128, 1024], F32))
        qsF = {k: EF(nc.sbuf_tensor(f"qF_{k}", [128, w], F32))[:, :]
               for k, w in [("sg", 1024), ("av", 1024), ("m2", 1024),
                            ("m4", 1024), ("t0", 1024), ("t1", 1024),
                            ("t2", 1024), ("amax", 32),
                            ("gbsc", 256), ("lsc", 256), ("blk", 384)]}
        qsFb = dict(qsF)
        qsFb["sg"] = sgFb[:, :]
        pgf = [EF(nc.psum_tensor(f"pgf{i}", [128, 512], F32)) for i in range(2)]
        puf = [EF(nc.psum_tensor(f"puf{i}", [128, 512], F32)) for i in range(2)]
        pqt = [EF(nc.psum_tensor(f"pqt{i}", [128, 128], BF16)) for i in range(2)]
        pd = [EF(nc.psum_tensor(f"pd{i}", [128, 512], F32)) for i in range(2)]

        if DEBUG:
            gp.wait_ge(s_v, a_end[0])
            gp.wait_ge(s_sp, a_end[2])
            gp.wait_ge(s_cc, wq_cc)
            SP(gp.dma_start(out=dbg_h_d, in_=xh_d))
            SP(gp.dma_start(out=dbg_wg_d, in_=wgqt_d))
            SP(gp.dma_start(out=dbg_wd_d, in_=wdqt_d))
            SP(gp.dma_start(out=dbg_xnq_d, in_=xnqt_d))
        out_sp = []
        for H in HALVES:
            for e in range(NE):
                for jj in range(8):
                    if H == 0 and e == 0 and jj == 0:
                        sy.wait_ge(s_v, a_end[0])
                        sy.wait_ge(s_t, a_end[1])
                        sy.wait_ge(s_sp, a_end[2])
                        sy.wait_ge(s_sc, a_end[3])
                    if H == 1 and e == 0 and jj == 0:
                        sy.wait_ge(s_t, h0_t_end)
                    LD(sy.dma_start(
                        out=xnqh[:, e * 1024 + 128 * jj:e * 1024 + 128 * (jj + 1)],
                        in_=xnqt_d[e, 8 * H + jj]))
            oinit_mark = []
            if H == 1:
                sy.wait_ge(s_sp, out_sp[7])
            for j in range(8):
                r0 = 1024 * H + 128 * j
                LO(sy.dma_start(out=oa[:, D * j:D * (j + 1)],
                                in_=xh_d[r0:r0 + 128, :]))
            v.wait_ge(s_o, cnt["o"])
            for j in range(8):
                r0 = 1024 * H + 128 * j
                if j >= 1:
                    sy.wait_ge(s_v, oinit_mark[j - 1])
                px = LX(0, sy.dma_start(out=xst[:, :],
                                        in_=xs_d[r0:r0 + 128, :]))
                v.wait_ge(s_xp[0], px)
                oinit_mark.append(V(v.tensor_tensor(oa[:, D * j:D * (j + 1)],
                                                    oa[:, D * j:D * (j + 1)],
                                                    xst[:, :], AL.add)))
            dn_mark = {}
            if H == 0:
                slab_te_done = None
                gup7_prev = None
                gu_mark = {}
                pgf_free = {}
                puf_free = {}
                sbuf_free = {}
                tq_free = [None]
                pqt_free = {}
                pd_free = {}
                pqm_c = [None]
                tq_ctr = [0]
            for s in (SLABS if SLABS is not None else range(16)):
                r_rk, sl_lo = divmod(s, 2)
                if gup7_prev is not None:
                    sy.wait_ge(s_t, gup7_prev)
                elif H == 1:
                    sy.wait_ge(s_t, h0_t_end)
                if H == 0 and s == (SLABS[0] if SLABS is not None else 0):
                    sy.wait_ge(s_cc, wq_cc)
                LO(sy.dma_start(
                    out=wgq[:, :],
                    in_=wgqt_d[r_rk, :, sl_lo * 8192:(sl_lo + 1) * 8192]))
                gu_ld = LO(sy.dma_start(
                    out=wuq[:, :],
                    in_=wuqt_d[r_rk, :, sl_lo * 8192:(sl_lo + 1) * 8192]))
                if slab_te_done is not None:
                    sy.wait_ge(s_t, slab_te_done)
                for ii in range(4):
                    for ht in range(NE):
                        ldc = LD(sy.dma_start(
                            out=wdq[:, ii * D + 128 * ht:ii * D + 128 * (ht + 1)],
                            in_=wdqt_d[ht, 4 * s + ii]))
                wd_ld = ldc

                def emit_gup(j):
                    te.wait_ge(s_o, gu_ld)
                    bk = j % 2
                    if pgf_free.get(bk) is not None:
                        te.wait_ge(s_sc, pgf_free[bk])
                        te.wait_ge(s_v, puf_free[bk])
                    last_u = None
                    for e in range(NE):
                        stat = xnqh[:, e * 1024 + 128 * j:e * 1024 + 128 * (j + 1)]
                        te.matmul(pgf[bk][:, :], stat,
                                  wgq[:, e * 512:(e + 1) * 512],
                                  start=(e == 0), stop=(e == NE - 1))
                        last_u = te.matmul(puf[bk][:, :], stat,
                                           wuq[:, e * 512:(e + 1) * 512],
                                           start=(e == 0), stop=(e == NE - 1))
                    gu_mark[j] = TE(last_u)

                def emit_pre(j):
                    bk = j % 2
                    sl = silu_t[:, bk * 512:(bk + 1) * 512]
                    it = int_t[:, bk * 512:(bk + 1) * 512]
                    sc.wait_ge(s_t, gu_mark[j])
                    if sbuf_free.get(bk) is not None:
                        sc.wait_ge(s_v, sbuf_free[bk])
                    pgf_free[bk] = SC(sc.activation(sl, pgf[bk][:, :], ACT.Silu))
                    v.wait_ge(s_sc, pgf_free[bk])
                    im = V(v.tensor_tensor(it, sl, puf[bk][:, :], AL.mult))
                    puf_free[bk] = im
                    sbuf_free[bk] = im

                def emit_q(p):
                    if tq_free[0] is not None:
                        v.wait_ge(s_t, tq_free[0])
                    pqm_c[0] = quant(int_t[:, 0:1024], tq_t[:, 0:1024], 3,
                                     (qsF, qsFb)[p % 2], tie=True,
                                     x_guard=sbuf_free[1])

                def emit_tail_te(j):
                    bk = j % 2
                    tq = tq_t[:, bk * 512:(bk + 1) * 512]
                    te.wait_ge(*pqm_c[0])
                    ev_last = None
                    for ii in range(4):
                        n = tq_ctr[0]
                        tq_ctr[0] += 1
                        if pqt_free.get(n % 2) is not None:
                            te.wait_ge(s_sc, pqt_free[n % 2])
                        tc = TE(te.transpose(pqt[n % 2][:, :],
                                             tq[:, 128 * ii:128 * (ii + 1)],
                                             eyeb_s[:, :]))
                        if bk == 1 and ii == 3:
                            tq_free[0] = tc
                        sc.wait_ge(s_t, tc)
                        ev_last = SC(sc.activation(
                            tqT[:, ii * 1024 + 128 * j:ii * 1024 + 128 * (j + 1)],
                            pqt[n % 2][:, :], ACT.Identity))
                        pqt_free[n % 2] = ev_last
                    te.wait_ge(s_sc, ev_last)
                    te.wait_ge(s_ld, wd_ld)
                    for cgrp in range(4):
                        dbk = cgrp % 2
                        if pd_free.get(dbk) is not None:
                            te.wait_ge(s_v, pd_free[dbk])
                        last = None
                        for ii in range(4):
                            last = te.matmul(
                                pd[dbk][:, :],
                                tqT[:, ii * 1024 + 128 * j:ii * 1024 + 128 * (j + 1)],
                                wdq[:, ii * D + 512 * cgrp:ii * D + 512 * (cgrp + 1)],
                                start=(ii == 0), stop=(ii == 3))
                        dtc = TE(last)
                        v.wait_ge(s_t, dtc)
                        dn_mark[(j, cgrp)] = V(v.tensor_tensor(
                            oa[:, D * j + 512 * cgrp:D * j + 512 * (cgrp + 1)],
                            oa[:, D * j + 512 * cgrp:D * j + 512 * (cgrp + 1)],
                            pd[dbk][:, :], AL.add))
                        pd_free[dbk] = dn_mark[(j, cgrp)]

                emit_gup(0)
                emit_gup(1)
                for p in range(4):
                    emit_pre(2 * p)
                    if 2 * p + 2 < 8:
                        emit_gup(2 * p + 2)
                    emit_pre(2 * p + 1)
                    if 2 * p + 3 < 8:
                        emit_gup(2 * p + 3)
                    emit_q(p)
                    emit_tail_te(2 * p)
                    emit_tail_te(2 * p + 1)
                slab_te_done = cnt["t"]
                gup7_prev = gu_mark[7]
            if H == 0:
                h0_t_end = cnt["t"]
            out_sp = []
            for j in range(8):
                gp.wait_ge(s_v, dn_mark[(j, 3)])
                for cg in range(3):
                    gp.wait_ge(s_v, dn_mark[(j, cg)])
                r0 = 1024 * H + 128 * j
                out_sp.append(SP(gp.dma_start(out=out_d[r0:r0 + 128, :],
                                              in_=oa[:, D * j:D * (j + 1)])))
        if DEBUG:
            gp.wait_ge(s_v, cnt["v"])
            SP(gp.dma_start(out=dbg_q_d[:, 0:512], in_=qsF["sg"][:, 0:512]))
            SP(gp.dma_start(out=dbg_q_d[:, 512:1024], in_=qsF["av"][:, 0:512]))
            SP(gp.dma_start(out=dbg_q_d[:, 1024:1536], in_=qsF["t1"][:, 0:512]))
            SP(gp.dma_start(out=dbg_q_d[:, 1536:2048], in_=qsF["t0"][:, 0:512]))
            SP(gp.dma_start(out=dbg_q_d[:, 2048:2064], in_=qsF["amax"][:, 0:16]))
            SP(gp.dma_start(out=dbg_q_d[:, 2064:2576], in_=qsF["gbsc"]))
            SP(gp.dma_start(out=dbg_q_d[:, 2576:3344], in_=qsF["blk"]))
            SP(gp.dma_start(out=dbg_q_d[:, 3344:3856], in_=qsF["m4"][:, 0:512]))
            SP(gp.dma_start(out=dbg_int_d, in_=int_t[:, 512:1024]))
            SP(gp.dma_start(out=dbg_silu_d, in_=silu_t[:, 512:1024]))
            SP(gp.dma_start(out=dbg_tq_d, in_=tq_t[:, 512:1024]))
            SP(gp.dma_start(out=dbg_tqt2_d, in_=tqT[:, :]))
        sy.wait_ge(s_sp, cnt["sp"])
        gp.wait_ge(s_sp, cnt["sp"])
        sF.close()

    @blk.vector
    def _(v):
        body(v)

    perm.close()
    return nc


_NC_CACHE = [None]


def kernel(L0_attn_x, L0_attn_W_in, L0_attn_a_param,
           L0_ffn_w_gate, L0_ffn_w_up, L0_ffn_w_down):
    x = np.ascontiguousarray(np.asarray(L0_attn_x, dtype=np.float32))
    W_in = np.ascontiguousarray(np.asarray(L0_attn_W_in, dtype=np.float32))
    a_param = np.ascontiguousarray(np.asarray(L0_attn_a_param, dtype=np.float32))
    wg = np.ascontiguousarray(np.asarray(L0_ffn_w_gate, dtype=np.float32))
    wu = np.ascontiguousarray(np.asarray(L0_ffn_w_up, dtype=np.float32))
    wd = np.ascontiguousarray(np.asarray(L0_ffn_w_down, dtype=np.float32))

    WT = np.ascontiguousarray(W_in.T)
    wint = np.ascontiguousarray(
        WT.reshape(NE, 128, NF, 128).transpose(2, 1, 0, 3)
        .reshape(NF, 128, NE * 128))
    afull = np.ascontiguousarray(
        np.tile(a_param, NE).reshape(NE, 128).T)
    eye = np.eye(128, dtype=np.float32)
    cmasks = []
    for c in range(NCORES):
        m = np.zeros((128, 8), dtype=np.float32)
        if c % 2 == 1:
            m[:, c - 1] = 1.0
        cmasks.append(m)

    if _NC_CACHE[0] is None:
        _NC_CACHE[0] = build_program()
    nc = _NC_CACHE[0]

    in_maps = []
    for c in range(NCORES):
        b, half = divmod(c, 2)
        in_maps.append({
            "xs": np.ascontiguousarray(x[b, half * T:(half + 1) * T, :]),
            "wint": wint,
            "wg": np.ascontiguousarray(wg[c * (FI // 8):(c + 1) * (FI // 8)]),
            "wu": np.ascontiguousarray(wu[c * (FI // 8):(c + 1) * (FI // 8)]),
            "wd": np.ascontiguousarray(wd[c * (D // 8):(c + 1) * (D // 8)]),
            "afull": afull, "cmask": cmasks[c], "eye": eye,
        })
    res = run_bass_kernel_spmd(nc, in_maps, core_ids=list(range(NCORES)))
    out = np.empty((4, 2 * T, D), dtype=np.float32)
    for c in range(NCORES):
        b, half = divmod(c, 2)
        out[b, half * T:(half + 1) * T, :] = np.asarray(res.results[c]["out"])
    return out
